# version 02
# baseline (speedup 1.0000x reference)
import contextlib
import numpy as np

import concourse.bass as bass
import concourse.mybir as mybir
from concourse.bass_utils import run_bass_kernel_spmd

F32 = mybir.dt.float32
BF16 = mybir.dt.bfloat16
AF = mybir.ActivationFunctionType
ALU = mybir.AluOpType
AX = mybir.AxisListType

TOK = 2048
NJC = 8
C = 1024
NT = 2050
EXT = 2049
DFF = 4096
DK = 256
NHEAD = 4
NEXP = 8
EPS_LN = 1e-5
EPS_RMS = 1e-6
POOL_BYTES = 145 * 1024

CH_TOK = [(0, 512), (512, 512), (1024, 512), (1536, 512)]
CH_EXT = CH_TOK + [(2048, 1)]
CH_NT = CH_TOK + [(2048, 2)]


def _chunks(n, size=512):
    out, o = [], 0
    while o < n:
        s = min(size, n - o)
        out.append((o, s))
        o += s
    return out


class Pool:

    def __init__(self, nc):
        a0 = nc._sbuf_addr_for_side("left")
        self.base = (a0 + 63) // 64 * 64
        pad = self.base - a0
        nc.alloc_sbuf_tensor("poolguard", [128, (POOL_BYTES + pad + 3) // 4],
                             F32)
        self.nc = nc
        self.size = POOL_BYTES
        self.regions = {}
        self.uid = 0
        self.peak = 0

    def alloc(self, name, shape, dtype):
        per = int(np.prod(shape[1:])) * (4 if dtype == F32 else 2)
        per = (per + 63) // 64 * 64
        taken = sorted(self.regions.values())
        gaps = []
        cur = 0
        for (o, s) in taken:
            if o > cur:
                gaps.append((cur, o - cur))
            cur = max(cur, o + s)
        if cur < self.size:
            gaps.append((cur, self.size - cur))
        fit = [g for g in gaps if g[1] >= per]
        assert fit, "pool overflow: %s needs %d (gaps %s)" % (name, per, gaps)
        off = min(fit, key=lambda g: g[1])[0]
        assert name not in self.regions, name
        self.regions[name] = (off, per)
        self.peak = max(self.peak, off + per)
        self.uid += 1
        return self.nc.alloc_sbuf_tensor_at(
            "pl_%s_%d" % (name, self.uid), shape, dtype,
            offset=self.base + off)

    def free(self, *names):
        for n in names:
            self.regions.pop(n)


class KB:
    def __init__(self, taps=(), stop=None):
        self.stop = stop
        self.nc = bass.Bass("TRN2", target_bir_lowering=False)
        self.main = contextlib.ExitStack()
        self.sems = {}
        self.cnt = {}
        self.lastw = {}
        self.maxw = {}
        self.taps = set(taps)

    def sb(self, name, shape, dtype):
        return self.main.enter_context(self.nc.sbuf_tensor(name, shape,
                                                           dtype))

    def sem(self, name):
        if name not in self.sems:
            self.sems[name] = self.main.enter_context(self.nc.semaphore(name))
            self.cnt[name] = 0
        return self.sems[name]

    def inc(self, name, ins, n=1):
        ins.then_inc(self.sem(name), n)
        self.cnt[name] += n
        return self.cnt[name]

    def w(self, eng, engname, semname, val=None):
        if semname not in self.cnt:
            return
        if val is None:
            val = self.cnt[semname]
        if val <= 0:
            return
        key = (engname, semname)
        if self.lastw.get(key, 0) >= val:
            return
        self.lastw[key] = val
        self.maxw[semname] = max(self.maxw.get(semname, 0), val)
        eng.wait_ge(self.sem(semname), val)

    def waitbar(self, eng, engname, snap=None):
        if snap is None:
            snap = dict(self.cnt)
        for k, v in snap.items():
            self.w(eng, engname, k, v)

    def bar(self):
        return dict(self.cnt)

    def fence(self, eng, engname):
        ins = eng.engine_nop() if hasattr(eng, "engine_nop") else eng.nop()
        self.inc(engname, ins)
        eng.wait_ge(self.sem(engname), self.cnt[engname])
        self.lastw[(engname, engname)] = self.cnt[engname]

    def mmchain(self, name, items, kdepth, lhsT_fn, rhs_fn, evac_fn,
                consumer="v", pre_snap=None, c_pre_snap=None,
                psum_shape=(128, 512)):
        nc = self.nc
        es = contextlib.ExitStack()
        ps = [es.enter_context(nc.psum_tensor("ps_%s_%d" % (name, i),
                                              list(psum_shape), F32))
              for i in range(2)]
        self.sem(consumer)
        self.sem("t")
        cbase = self.cnt[consumer]
        tbase = self.cnt["t"]

        with nc.Block("mc_" + name) as blk:
            @blk.tensor
            def _(te):
                if pre_snap:
                    self.waitbar(te, "t", pre_snap)
                for idx, (jc, o, n) in enumerate(items):
                    if idx >= 2:
                        self.w(te, "t", consumer, cbase + idx - 1)
                    p = ps[idx % 2]
                    for ci in range(kdepth):
                        ins = te.matmul(p[:, 0:n], lhsT_fn(jc, ci),
                                        rhs_fn(jc, ci, o, n),
                                        start=(ci == 0),
                                        stop=(ci == kdepth - 1))
                    self.inc("t", ins)

            def consume(eng):
                if c_pre_snap:
                    self.waitbar(eng, consumer, c_pre_snap)
                for idx, (jc, o, n) in enumerate(items):
                    self.w(eng, consumer, "t", tbase + idx + 1)
                    ins = evac_fn(eng, ps[idx % 2][:, 0:n], jc, o, n)
                    self.inc(consumer, ins)

            if consumer == "v":
                @blk.vector
                def _(v):
                    consume(v)
            else:
                @blk.scalar
                def _(sc):
                    consume(sc)
        es.close()

    def ln_stats(self, name, ncols, xbf, sqbf, mean_b, rstd_b, eps,
                 subtract_mean=True, pre_snap=None):
        nc, P = self.nc, self.P
        mrow = self.T1[0:1, 0:ncols]
        vrow = self.T2[0:1, 0:ncols]
        srow = self.T3[0:1, 0:ncols]
        rrow = self.MB[0:1, 0:ncols]
        chs = _chunks(ncols)
        es = contextlib.ExitStack()
        pssA = [es.enter_context(nc.psum_tensor("lsA%s%d" % (name, i),
                                                [1, 512], F32))
                for i in range(2)]
        pssB = [es.enter_context(nc.psum_tensor("lsB%s%d" % (name, i),
                                                [1, 512], F32))
                for i in range(2)]
        psbA = [es.enter_context(nc.psum_tensor("lbA%s%d" % (name, i),
                                                [128, 512], F32))
                for i in range(2)]
        psbB = [es.enter_context(nc.psum_tensor("lbB%s%d" % (name, i),
                                                [128, 512], F32))
                for i in range(2)]
        self.sem("t"), self.sem("v"), self.sem("s")
        tbase = self.cnt["t"]
        vbase = self.cnt["v"]

        with nc.Block("ls_" + name) as blk:
            @blk.tensor
            def _(te):
                if pre_snap:
                    self.waitbar(te, "t", pre_snap)
                for ci, (o, n) in enumerate(chs):
                    if ci >= 2:
                        self.w(te, "t", "v", vbase + ci - 1)
                    for jc in range(NJC):
                        if subtract_mean:
                            te.matmul(pssA[ci % 2][:, 0:n], P["onescol_b"][:],
                                      xbf[:, jc, o:o + n], start=(jc == 0),
                                      stop=(jc == NJC - 1))
                        ins = te.matmul(pssB[ci % 2][:, 0:n],
                                        P["onescol_b"][:],
                                        sqbf[:, jc, o:o + n],
                                        start=(jc == 0), stop=(jc == NJC - 1))
                    self.inc("t", ins)

            @blk.vector
            def _(v):
                for ci, (o, n) in enumerate(chs):
                    self.w(v, "v", "t", tbase + ci + 1)
                    if subtract_mean:
                        v.tensor_scalar_mul(mrow[:, o:o + n],
                                            pssA[ci % 2][:, 0:n], 1.0 / C)
                    self.inc("v", v.tensor_scalar_mul(
                        vrow[:, o:o + n], pssB[ci % 2][:, 0:n], 1.0 / C))
                self.fence(v, "v")
                if subtract_mean:
                    v.tensor_mul(srow[:, :], mrow[:, :], mrow[:, :])
                    self.fence(v, "v")
                    v.tensor_sub(vrow[:, :], vrow[:, :], srow[:, :])
                    self.fence(v, "v")
                self.inc("v", v.tensor_scalar_add(vrow[:, :], vrow[:, :],
                                                  float(eps)))
            veps = self.cnt["v"]

            @blk.scalar
            def _(sc):
                self.w(sc, "s", "v", veps)
                self.inc("s", sc.activation(srow[:, :], vrow[:, :], AF.Sqrt))
            sq_done = self.cnt["s"]

            @blk.vector
            def _(v):
                self.w(v, "v", "s", sq_done)
                v.reciprocal(rrow[:, :], srow[:, :])
                self.fence(v, "v")
                v.tensor_mul(srow[:, :], rrow[:, :], rrow[:, :])
                self.fence(v, "v")
                v.tensor_mul(srow[:, :], srow[:, :], vrow[:, :])
                self.fence(v, "v")
                v.tensor_scalar(srow[:, :], srow[:, :], -0.5, 1.5,
                                ALU.mult, ALU.add)
                self.fence(v, "v")
                self.inc("v", v.tensor_mul(vrow[:, :], rrow[:, :],
                                           srow[:, :]))
            rr_done = self.cnt["v"]
            vbase2 = self.cnt["v"]
            tbase2 = self.cnt["t"]

            @blk.tensor
            def _(te):
                self.w(te, "t", "v", rr_done)
                for ci, (o, n) in enumerate(chs):
                    if ci >= 2:
                        self.w(te, "t", "v", vbase2 + ci - 1)
                    if subtract_mean:
                        te.matmul(psbA[ci % 2][:, 0:n], P["onesrow_f"][:],
                                  mrow[:, o:o + n], start=True, stop=True)
                    ins = te.matmul(psbB[ci % 2][:, 0:n], P["onesrow_f"][:],
                                    vrow[:, o:o + n], start=True, stop=True)
                    self.inc("t", ins)

            @blk.vector
            def _(v):
                for ci, (o, n) in enumerate(chs):
                    self.w(v, "v", "t", tbase2 + ci + 1)
                    if subtract_mean:
                        v.tensor_copy(mean_b[:, o:o + n], psbA[ci % 2][:, 0:n])
                    self.inc("v", v.tensor_copy(rstd_b[:, o:o + n],
                                                psbB[ci % 2][:, 0:n]))
        es.close()

    def mk_sq(self, name, src_f32, xbf, sqbf, ncols, pre_snap=None):
        with self.nc.Block("sq_" + name) as blk:
            @blk.vector
            def _(v):
                if pre_snap:
                    self.waitbar(v, "v", pre_snap)
                for jc in range(NJC):
                    v.tensor_copy(xbf[:, jc, :], src_f32(jc))
                    self.inc("v", v.tensor_mul(sqbf[:, jc, :],
                                               xbf[:, jc, :], xbf[:, jc, :]))

    def mk_sq_dram(self, name, dram_slice_fn, xbf, sqbf, ncols,
                   pre_snap=None):
        with self.nc.Block("sqd_" + name) as blk:
            lds = []
            vb0 = self.cnt.get("v", 0)

            @blk.sync
            def _(sp):
                if pre_snap:
                    self.waitbar(sp, "sp", pre_snap)
                for jc in range(NJC):
                    if jc >= 1:
                        self.w(sp, "sp", "v", vb0 + jc)
                    lds.append(self.inc("spd", sp.dma_start(
                        out=self.T3[:, 0:ncols], in_=dram_slice_fn(jc)), 16))

            @blk.vector
            def _(v):
                if pre_snap:
                    self.waitbar(v, "v", pre_snap)
                for jc in range(NJC):
                    self.w(v, "v", "spd", lds[jc])
                    v.tensor_copy(xbf[:, jc, :], self.T3[:, 0:ncols])
                    self.inc("v", v.tensor_mul(sqbf[:, jc, :],
                                               xbf[:, jc, :], xbf[:, jc, :]))

    def load_weight(self, name, dram, nout, dest_bf, pre_snap=None,
                    row_off=0, stride=None):
        nc = self.nc
        if stride is None:
            stride = nout
        stg = self.pool.alloc("stg_" + name, [128, NJC, min(nout, 512)], F32)
        nchunk = _chunks(nout, 512)
        self.sem("spd"), self.sem("s")
        loads = []
        sbase = self.cnt["s"]

        with nc.Block("lw_" + name) as blk:
            @blk.sync
            def _(sp):
                if pre_snap:
                    self.waitbar(sp, "sp", pre_snap)
                for i, (o, n) in enumerate(nchunk):
                    if i >= 1:
                        self.w(sp, "sp", "s", sbase + i)
                    ap = bass.AP(dram, row_off * stride + o,
                                 [[stride, 128], [128 * stride, NJC], [1, n]])
                    loads.append(self.inc("spd", sp.dma_start(
                        out=stg[:, :, 0:n], in_=ap), 16))

            @blk.scalar
            def _(sc):
                for i, (o, n) in enumerate(nchunk):
                    self.w(sc, "s", "spd", loads[i])
                    self.inc("s", sc.copy(dest_bf[:, :, o:o + n],
                                          stg[:, :, 0:n]))
        self.pool.free("stg_" + name)

    def tap(self, name, src_ap_fn, shape, dtype):
        if name not in self.taps:
            return
        h = self.nc.dram_tensor("dbg_" + name, shape, dtype,
                                kind="ExternalOutput")
        with self.nc.Block("tap_" + name) as blk:
            @blk.sync
            def _(sp):
                self.waitbar(sp, "sp")
                self.inc("spd", sp.dma_start(out=h[:], in_=src_ap_fn()), 16)

    def build(self):
        self.declare_io()
        phases = [("p0", self.p0_setup), ("p1", self.p1_lnA),
                  ("p2", self.p2_wkv), ("p3", self.p3_ow),
                  ("p4", self.p4_ln2), ("p5", self.p5_cm),
                  ("p6", self.p6_x1), ("p7", self.p7_lnB),
                  ("p8", self.p8_qkv), ("p9", self.p9_attn),
                  ("p10", self.p10_wo), ("p11", self.p11_router),
                  ("p12", self.p12_moe), ("p13", self.p13_out)]
        done = []
        for name, fn in phases:
            fn()
            done.append(name)
            if self.stop == name or (self.stop or "").startswith(name + ""):
                if self.stop == name or (name == "p5" and
                                         self.stop in ("p5s", "p5k", "p5kk0")):
                    break
        if len(done) < len(phases):
            with self.nc.Block("dummyout") as blk:
                @blk.vector
                def _(v):
                    self.waitbar(v, "v")
                    self.inc("v", v.memset(self.T1[:, 0:TOK], 0.0))
                mz = self.cnt["v"]

                @blk.sync
                def _(sp):
                    self.w(sp, "sp", "v", mz)
                    for jc in range(NJC):
                        self.inc("spd", sp.dma_start(
                            out=self.D["out"][:, jc, :],
                            in_=self.T1[:, 0:TOK]), 16)
        with self.nc.Block("fin") as blk:
            @blk.sync
            def _(sp):
                self.waitbar(sp, "sp")
        for k, v in self.maxw.items():
            assert v <= self.cnt.get(k, 0), \
                "wait overshoot on %s: %d > %d" % (k, v, self.cnt.get(k, 0))
        return self.nc

    def declare_io(self):
        nc = self.nc
        D = {}
        D["x"] = nc.dram_tensor("x", [128, NJC, NT], F32, kind="ExternalInput")
        for nm in ["tmk", "tmv", "tmr", "ftmk", "ftmr", "ln1w", "ln1b",
                   "ln2w", "ln2b", "g1", "g2", "decay", "tfirst"]:
            D[nm] = nc.dram_tensor("pv_" + nm, [128, NJC], F32,
                                   kind="ExternalInput")
        D["sel"] = nc.dram_tensor("sel", [128, 4], F32, kind="ExternalInput")
        D["bm"] = nc.dram_tensor("bm", [128, 2], F32, kind="ExternalInput")
        D["ident"] = nc.dram_tensor("ident", [128, 128], F32,
                                    kind="ExternalInput")
        D["ebias"] = nc.dram_tensor("ebias", [1, 8], F32, kind="ExternalInput")
        for nm, sh in [("wk", [C, C]), ("wv", [C, C]), ("wr", [C, C]),
                       ("wo", [C, C]), ("fkw", [C, DFF]), ("fvw", [DFF, C]),
                       ("frw", [C, C]), ("wq", [C, C]), ("wkc", [C, DK]),
                       ("wvc", [C, DK]), ("wwo", [C, C]), ("wwr", [C, 8]),
                       ("s1", [C, C]), ("s2", [C, C]), ("s3", [C, C]),
                       ("e1", [NEXP, C, C]), ("e2", [NEXP, C, C]),
                       ("e3", [NEXP, C, C])]:
            D[nm] = nc.dram_tensor("w_" + nm, sh, F32, kind="ExternalInput")
        D["out"] = nc.dram_tensor("out", [128, NJC, TOK], F32,
                                  kind="ExternalOutput")
        for nm, sh, dt in [("dr_y0", [128, NJC, NT], F32),
                           ("dr_ek", [128, NJC, EXT], F32),
                           ("dr_v", [128, NJC, EXT], F32),
                           ("dr_y1", [128, NJC, EXT], F32),
                           ("dr_y2", [128, NJC, TOK], F32),
                           ("dr_x1", [128, NJC, TOK], F32),
                           ("dr_y0b", [128, NJC, TOK], F32),
                           ("dr_xb", [128, NJC, TOK], F32),
                           ("dr_fkw", [32, 128, 1024], BF16),
                           ("dr_fvw", [NJC, 128, 4096], BF16),
                           ("dr_brow", [TOK], F32),
                           ("dr_comb", [NEXP, TOK], F32),
                           ("cc_wkv_in", [3, 128, NJC], F32),
                           ("cc_wkv_out", [4, 3, 128, NJC], F32)]:
            D[nm] = nc.dram_tensor(nm, sh, dt)
        NAT = 2 * 128 * TOK + 16 * 128 * DK
        self.NAT = NAT
        D["cc_at_in"] = nc.dram_tensor("cc_at_in", [NAT], BF16)
        D["cc_at_out"] = nc.dram_tensor("cc_at_out", [4 * NAT], BF16)
        self.D = D

        P = {}
        for nm in ["tmk", "tmv", "tmr", "ftmk", "ftmr", "ln1w", "ln1b",
                   "ln2w", "ln2b", "g1", "g2", "decay", "tfirst"]:
            P[nm] = self.sb("s_" + nm, [128, NJC], F32)
        P["sel"] = self.sb("s_sel", [128, 4], F32)
        P["bm"] = self.sb("s_bm", [128, 2], F32)
        P["ident"] = self.sb("s_ident", [128, 128], F32)
        P["ebias"] = self.sb("s_ebias", [1, 8], F32)
        for nm in ["wvec", "lam", "wL", "kap", "negkap", "euk", "sumA",
                   "sumB", "initA", "initB", "lamc"]:
            P[nm] = self.sb("s_" + nm, [128, NJC if nm != "lamc" else 1], F32)
        P["onescol_b"] = self.sb("s_onescol_b", [128, 1], BF16)
        P["onesrow_f"] = self.sb("s_onesrow_f", [1, 128], F32)
        self.P = P
        self.MB = self.sb("s_MB", [128, 2052], F32)
        self.RB = self.sb("s_RB", [128, 2052], F32)
        self.T1 = self.sb("s_T1", [128, 2052], F32)
        self.T2 = self.sb("s_T2", [128, 2052], F32)
        self.T3 = self.sb("s_T3", [128, 2052], F32)
        for s in ("t", "v", "s", "g", "spd", "gd", "cc", "sp"):
            self.sem(s)
        self.pool = Pool(nc)

    def p0_setup(self):
        nc, D, P = self.nc, self.D, self.P
        with nc.Block("p0") as blk:
            @blk.sync
            def _(sp):
                for nm in ["tmk", "tmv", "tmr", "ftmk", "ftmr", "ln1w",
                           "ln1b", "ln2w", "ln2b", "g1", "g2", "decay",
                           "tfirst", "sel", "bm", "ident", "ebias"]:
                    self.inc("spd", sp.dma_start(out=P[nm][:],
                                                 in_=D[nm][:]), 16)
            pv_ld = self.cnt["spd"]

            @blk.scalar
            def _(sc):
                self.w(sc, "s", "spd", pv_ld)
                sc.activation(P["wvec"][:], P["decay"][:], AF.Exp)
                self.fence(sc, "s")
                sc.mul(P["wvec"][:], P["wvec"][:], -1.0)
                self.fence(sc, "s")
                sc.activation(P["lam"][:], P["wvec"][:], AF.Exp)
                sc.activation(P["euk"][:], P["tfirst"][:], AF.Exp)
                self.inc("s", sc.mul(P["wL"][:], P["wvec"][:], 2048.0))

            @blk.vector
            def _(v):
                v.memset(P["onescol_b"][:], 1.0)
                self.inc("v", v.memset(P["onesrow_f"][:], 1.0))

    def p1_lnA(self):
        nc, D, P = self.nc, self.D, self.P
        pool = self.pool
        X = pool.alloc("X", [128, NJC, NT], F32)
        xbf = pool.alloc("xbf", [128, NJC, NT], BF16)
        sqbf = pool.alloc("sqbf", [128, NJC, NT], BF16)
        mb, rb = self.MB, self.RB

        with nc.Block("p1_load") as blk:
            @blk.sync
            def _(sp):
                self.inc("spd", sp.dma_start(out=X[:], in_=D["x"][:]), 16)
            x_ld = self.cnt["spd"]

            @blk.vector
            def _(v):
                self.w(v, "v", "spd", x_ld)
                for jc in range(NJC):
                    v.tensor_copy(xbf[:, jc, :], X[:, jc, :])
                    self.inc("v", v.tensor_mul(sqbf[:, jc, :],
                                               xbf[:, jc, :], xbf[:, jc, :]))
        self.ln_stats("p1s1", NT, xbf, sqbf, mb, rb, EPS_LN,
                      pre_snap=self.bar())

        with nc.Block("p1_y0") as blk:
            st = self.bar()

            @blk.vector
            def _(v):
                self.waitbar(v, "v", st)
                for jc in range(NJC):
                    v.tensor_sub(X[:, jc, :], X[:, jc, :], mb[:, 0:NT])
                    v.tensor_mul(X[:, jc, :], X[:, jc, :], rb[:, 0:NT])
                    v.tensor_copy(xbf[:, jc, :], X[:, jc, :])
                    self.inc("v", v.tensor_mul(sqbf[:, jc, :],
                                               xbf[:, jc, :], xbf[:, jc, :]))
            y0_done = self.cnt["v"]

            @blk.sync
            def _(sp):
                self.w(sp, "sp", "v", y0_done)
                self.inc("spd", sp.dma_start(out=D["dr_y0"][:], in_=X[:]), 16)
        self.tap("y0", lambda: X[:], [128, NJC, NT], F32)
        self.ln_stats("p1s2", NT, xbf, sqbf, mb, rb, EPS_LN,
                      pre_snap=self.bar())
        y0_spilled = self.bar()
        pool.free("X", "xbf", "sqbf")

        self.xk = pool.alloc("xk", [128, NJC, EXT], BF16)
        self.xv = pool.alloc("xv", [128, NJC, EXT], BF16)
        self.xr = pool.alloc("xr", [128, NJC, EXT], BF16)
        with nc.Block("p1_mix") as blk:
            st = self.bar()
            lds = []
            vb0 = self.cnt["v"]

            @blk.sync
            def _(sp):
                self.waitbar(sp, "sp", st)
                for jc in range(NJC):
                    if jc >= 1:
                        self.w(sp, "sp", "v", vb0 + 2 * jc)
                    lds.append(self.inc("spd", sp.dma_start(
                        out=self.T1[:, 0:NT], in_=D["dr_y0"][:, jc, :]), 16))

            @blk.vector
            def _(v):
                self.waitbar(v, "v", st)
                ht = self.T2[:, 0:NT]
                df = self.T3[:, 0:EXT]
                for jc in range(NJC):
                    self.w(v, "v", "spd", lds[jc])
                    v.tensor_sub(ht, self.T1[:, 0:NT], mb[:, 0:NT])
                    v.tensor_mul(ht, ht, rb[:, 0:NT])
                    v.tensor_scalar(ht, ht, P["ln1w"][:, jc:jc + 1],
                                    P["ln1b"][:, jc:jc + 1], ALU.mult,
                                    ALU.add)
                    v.tensor_mul(self.T2[:, 1:2], self.T2[:, 1:2],
                                 P["bm"][:, 0:1])
                    self.fence(v, "v")
                    v.tensor_sub(df, self.T2[:, 1:NT], self.T2[:, 0:NT - 1])
                    v.scalar_tensor_tensor(self.xk[:, jc, :], df,
                                           P["tmk"][:, jc:jc + 1],
                                           self.T2[:, 0:NT - 1],
                                           ALU.mult, ALU.add)
                    v.scalar_tensor_tensor(self.xv[:, jc, :], df,
                                           P["tmv"][:, jc:jc + 1],
                                           self.T2[:, 0:NT - 1],
                                           ALU.mult, ALU.add)
                    self.inc("v", v.scalar_tensor_tensor(
                        self.xr[:, jc, :], df, P["tmr"][:, jc:jc + 1],
                        self.T2[:, 0:NT - 1], ALU.mult, ALU.add))
        self.p1_end = self.bar()

    def p2_wkv(self):
        nc, D, P = self.nc, self.D, self.P
        pool = self.pool

        wk_b = pool.alloc("wk", [128, NJC, C], BF16)
        self.load_weight("wk", D["wk"], C, wk_b, pre_snap=self.p1_end)
        kt = pool.alloc("kt", [128, EXT], F32)
        ek = pool.alloc("ek", [128, EXT], F32)
        for jc in range(NJC):
            guard = self.bar()
            self.mmchain(
                "p2k%d" % jc, [(jc, o, n) for (o, n) in CH_EXT], NJC,
                lambda jc_, ci: wk_b[:, ci, jc_ * 128:(jc_ + 1) * 128],
                lambda jc_, ci, o, n: self.xk[:, ci, o:o + n],
                lambda v, pcol, jc_, o, n: v.tensor_copy(kt[:, o:o + n],
                                                         pcol),
                consumer="v", pre_snap=guard, c_pre_snap=guard)
            with nc.Block("p2k_%d" % jc) as blk:
                @blk.vector
                def _(v, jc=jc):
                    self.waitbar(v, "v")
                    v.tensor_scalar(kt[:, 0:1], kt[:, 0:1], P["bm"][:, 0:1],
                                    P["bm"][:, 1:2], ALU.mult, ALU.add)
                    self.fence(v, "v")
                    self.inc("v", v.reduce_max(P["kap"][:, jc:jc + 1],
                                               kt[:, :], axis=AX.X))
                    self.fence(v, "v")
                    self.inc("v", v.tensor_scalar_mul(
                        P["negkap"][:, jc:jc + 1],
                        P["kap"][:, jc:jc + 1], -1.0))
                kv_done = self.cnt["v"]

                @blk.scalar
                def _(sc, jc=jc, kv_done=kv_done):
                    self.w(sc, "s", "spd", self.cnt.get("spd", 0))
                    self.w(sc, "s", "v", kv_done)
                    self.inc("s", sc.activation(
                        ek[:, :], kt[:, :], AF.Exp,
                        bias=P["negkap"][:, jc:jc + 1]))
                ek_done = self.cnt["s"]

                @blk.sync
                def _(sp, jc=jc, ek_done=ek_done):
                    self.w(sp, "sp", "s", ek_done)
                    self.inc("spd", sp.dma_start(out=D["dr_ek"][:, jc, :],
                                                 in_=ek[:, :]), 16)
        ka = self.bar()
        pool.free("wk", "kt", "ek")
        pool.free("xk")

        wv_b = pool.alloc("wv", [128, NJC, C], BF16)
        self.load_weight("wv", D["wv"], C, wv_b, pre_snap=ka)
        vt = pool.alloc("vt", [128, EXT], F32)
        ek2 = pool.alloc("ek2", [128, EXT], F32)
        ah = pool.alloc("ah", [128, TOK], F32)
        bh = pool.alloc("bh", [128, TOK], F32)
        ekv = pool.alloc("ekv", [128, TOK], F32)
        for jc in range(NJC):
            guard = self.bar()
            with nc.Block("p2vl_%d" % jc) as blk:
                @blk.sync
                def _(sp, jc=jc, guard=guard):
                    self.waitbar(sp, "sp", guard)
                    self.inc("spd", sp.dma_start(
                        out=ek2[:, :], in_=D["dr_ek"][:, jc, :]), 16)
            self.mmchain(
                "p2v%d" % jc, [(jc, o, n) for (o, n) in CH_EXT], NJC,
                lambda jc_, ci: wv_b[:, ci, jc_ * 128:(jc_ + 1) * 128],
                lambda jc_, ci, o, n: self.xv[:, ci, o:o + n],
                lambda v, pcol, jc_, o, n: v.tensor_copy(vt[:, o:o + n],
                                                         pcol),
                consumer="v", pre_snap=guard, c_pre_snap=guard)
            ld = self.cnt["spd"]
            with nc.Block("p2vs_%d" % jc) as blk:
                @blk.vector
                def _(v, jc=jc, ld=ld):
                    self.w(v, "v", "spd", ld)
                    v.tensor_copy(P["lamc"][:, :], P["lam"][:, jc:jc + 1])
                    v.tensor_mul(ekv[:, :], ek2[:, 0:TOK], vt[:, 0:TOK])
                    self.fence(v, "v")
                    lb = P["lamc"].broadcast_to([128, 1024])
                    self.inc("v", v.tensor_tensor_scan(
                        bh[:, 0:1024], lb, ek2[:, 0:1024], 0.0,
                        ALU.mult, ALU.add))
                    self.inc("v", v.tensor_tensor_scan(
                        ah[:, 0:1024], lb, ekv[:, 0:1024], 0.0,
                        ALU.mult, ALU.add))
                    self.fence(v, "v")
                    self.inc("v", v.tensor_tensor_scan(
                        bh[:, 1024:2048], lb, ek2[:, 1024:2048],
                        bh[:, 1023:1024], ALU.mult, ALU.add))
                    self.inc("v", v.tensor_tensor_scan(
                        ah[:, 1024:2048], lb, ekv[:, 1024:2048],
                        ah[:, 1023:1024], ALU.mult, ALU.add))
                    self.fence(v, "v")
                    v.tensor_copy(P["sumA"][:, jc:jc + 1], ah[:, 2047:2048])
                    self.inc("v", v.tensor_copy(P["sumB"][:, jc:jc + 1],
                                                bh[:, 2047:2048]))
                p1v = self.cnt["v"]

                @blk.sync
                def _(sp, jc=jc, p1v=p1v):
                    self.w(sp, "sp", "v", p1v)
                    self.inc("spd", sp.dma_start(out=D["dr_v"][:, jc, :],
                                                 in_=vt[:, :]), 16)

        self._p2_gather_combine()
        pool.free("wv", "vt", "ek2", "ah", "bh", "ekv", "xv")
        self._p2_pass2()
        pool.free("xr")
        self.p2_end = self.bar()

    def _p2_gather_combine(self):
        nc, D, P = self.nc, self.D, self.P
        pool = self.pool
        gA = pool.alloc("gA", [128, 12 * NJC], F32)
        pre = pool.alloc("pre", [128, 12 * NJC], F32)
        cmb = pool.alloc("cmb", [128, 5 * NJC], F32)
        snap = self.bar()

        def A(r):
            return gA[:, (r * 3 + 0) * NJC:(r * 3 + 1) * NJC]

        def B(r):
            return gA[:, (r * 3 + 1) * NJC:(r * 3 + 2) * NJC]

        def M(r):
            return gA[:, (r * 3 + 2) * NJC:(r * 3 + 3) * NJC]

        def pA(r):
            return pre[:, (r * 3 + 0) * NJC:(r * 3 + 1) * NJC]

        def pB(r):
            return pre[:, (r * 3 + 1) * NJC:(r * 3 + 2) * NJC]

        def pM(r):
            return pre[:, (r * 3 + 2) * NJC:(r * 3 + 3) * NJC]

        with nc.Block("p2_ag") as blk:
            @blk.gpsimd
            def _(g):
                self.waitbar(g, "g", snap)
                for i, slab in enumerate([P["sumA"], P["sumB"], P["kap"]]):
                    self.inc("gd", g.dma_start(out=D["cc_wkv_in"][i],
                                               in_=slab[:, :]), 16)
                self.w(g, "g", "gd", self.cnt["gd"])
                self.inc("cc", g.collective_compute(
                    "AllGather", ALU.bypass,
                    replica_groups=[[0, 1, 2, 3], [4, 5, 6, 7]],
                    ins=[D["cc_wkv_in"][:]],
                    outs=[D["cc_wkv_out"][:]]))
                self.w(g, "g", "cc", self.cnt["cc"])
                for r in range(4):
                    for k in range(3):
                        self.inc("gd", g.dma_start(
                            out=gA[:, (r * 3 + k) * NJC:
                                    (r * 3 + k + 1) * NJC],
                            in_=D["cc_wkv_out"][r, k]), 16)
            gth = self.cnt["gd"]

        e1 = cmb[:, 0 * NJC:1 * NJC]
        e2 = cmb[:, 1 * NJC:2 * NJC]
        d1 = cmb[:, 2 * NJC:3 * NJC]
        t1 = cmb[:, 3 * NJC:4 * NJC]
        t2 = cmb[:, 4 * NJC:5 * NJC]
        with nc.Block("p2_cmb0") as blk:
            @blk.vector
            def _(v):
                self.w(v, "v", "gd", gth)
                v.memset(pA(0)[:, :], 0.0)
                v.memset(pB(0)[:, :], 0.0)
                v.memset(pM(0)[:, :], -1e38)
                v.tensor_copy(pA(1)[:, :], A(0))
                v.tensor_copy(pB(1)[:, :], B(0))
                self.inc("v", v.tensor_copy(pM(1)[:, :], M(0)))
        for r in (1, 2):
            with nc.Block("p2_cmb%d" % r) as blk:
                vp = self.cnt["v"]

                @blk.vector
                def _(v, r=r, vp=vp):
                    self.w(v, "v", "v", vp)
                    v.tensor_add(d1[:, :], pM(r)[:, :], P["wL"][:, :])
                    self.fence(v, "v")
                    v.tensor_max(pM(r + 1)[:, :], d1[:, :], M(r))
                    self.fence(v, "v")
                    v.tensor_sub(e1[:, :], d1[:, :], pM(r + 1)[:, :])
                    self.inc("v", v.tensor_sub(e2[:, :], M(r),
                                               pM(r + 1)[:, :]))
                ve = self.cnt["v"]

                @blk.scalar
                def _(sc, ve=ve):
                    self.w(sc, "s", "v", ve)
                    sc.activation(e1[:, :], e1[:, :], AF.Exp)
                    self.inc("s", sc.activation(e2[:, :], e2[:, :], AF.Exp))
                se = self.cnt["s"]

                @blk.vector
                def _(v, r=r, se=se):
                    self.w(v, "v", "s", se)
                    v.tensor_mul(t1[:, :], e1[:, :], pA(r)[:, :])
                    v.tensor_mul(t2[:, :], e2[:, :], A(r))
                    self.fence(v, "v")
                    v.tensor_add(pA(r + 1)[:, :], t1[:, :], t2[:, :])
                    v.tensor_mul(t1[:, :], e1[:, :], pB(r)[:, :])
                    v.tensor_mul(t2[:, :], e2[:, :], B(r))
                    self.fence(v, "v")
                    self.inc("v", v.tensor_add(pB(r + 1)[:, :],
                                               t1[:, :], t2[:, :]))
        with nc.Block("p2_sel") as blk:
            vp = self.cnt["v"]

            @blk.vector
            def _(v, vp=vp):
                self.w(v, "v", "v", vp)
                for dst, fld in ((P["initA"], pA), (P["initB"], pB),
                                 (t2, pM)):
                    dd = dst if dst is not t2 else t2
                    v.tensor_scalar_mul(dd[:, :], fld(0)[:, :],
                                        P["sel"][:, 0:1])
                    for r in (1, 2, 3):
                        v.scalar_tensor_tensor(dd[:, :], fld(r)[:, :],
                                               P["sel"][:, r:r + 1],
                                               dd[:, :], ALU.mult, ALU.add)
                self.inc("v", v.tensor_sub(e1[:, :], t2[:, :],
                                           P["kap"][:, :]))
            ve = self.cnt["v"]

            @blk.scalar
            def _(sc, ve=ve):
                self.w(sc, "s", "v", ve)
                self.inc("s", sc.activation(e1[:, :], e1[:, :], AF.Exp))
            se = self.cnt["s"]

            @blk.vector
            def _(v, se=se):
                self.w(v, "v", "s", se)
                v.tensor_mul(P["initA"][:, :], P["initA"][:, :], e1[:, :])
                self.inc("v", v.tensor_mul(P["initB"][:, :],
                                           P["initB"][:, :], e1[:, :]))
                self.fence(v, "v")
        if "wkvst" in self.taps:
            hst = self.nc.dram_tensor("dbg_wkvst", [7, 128, NJC], F32,
                                      kind="ExternalOutput")
            with nc.Block("tap_wkvst") as blk:
                @blk.sync
                def _(sp):
                    self.waitbar(sp, "sp")
                    for i, sl in enumerate([P["sumA"], P["sumB"], P["kap"],
                                            P["initA"], P["initB"],
                                            P["euk"], P["wvec"]]):
                        self.inc("spd", sp.dma_start(out=hst[i],
                                                     in_=sl[:, :]), 16)
            hga = self.nc.dram_tensor("dbg_wkvga", [12, 128, NJC], F32,
                                      kind="ExternalOutput")
            with nc.Block("tap_wkvga") as blk:
                @blk.sync
                def _(sp):
                    for r in range(4):
                        for k in range(3):
                            self.inc("spd", sp.dma_start(
                                out=hga[r * 3 + k],
                                in_=gA[:, (r * 3 + k) * NJC:
                                        (r * 3 + k + 1) * NJC]), 16)
            hpre = self.nc.dram_tensor("dbg_wkvpre", [12, 128, NJC], F32,
                                       kind="ExternalOutput")
            with nc.Block("tap_wkvpre") as blk:
                @blk.sync
                def _(sp):
                    for r in range(4):
                        for k in range(3):
                            self.inc("spd", sp.dma_start(
                                out=hpre[r * 3 + k],
                                in_=pre[:, (r * 3 + k) * NJC:
                                        (r * 3 + k + 1) * NJC]), 16)
        self.pool.free("gA", "pre", "cmb")

    def _p2_pass2(self):
        nc, D, P = self.nc, self.D, self.P
        pool = self.pool
        wr_b = pool.alloc("wr", [128, NJC, C], BF16)
        self.load_weight("wr", D["wr"], C, wr_b, pre_snap=self.bar())
        self.rwkv = pool.alloc("rwkv", [128, NJC, EXT], BF16)
        ek = pool.alloc("p2c_ek", [128, EXT], F32)
        vt = pool.alloc("p2c_vt", [128, EXT], F32)
        ah = pool.alloc("p2c_ah", [128, EXT], F32)
        bh = pool.alloc("p2c_bh", [128, EXT], F32)
        nm = pool.alloc("p2c_nm", [128, EXT], F32)

        for jc in range(NJC):
            guard = self.bar()
            with nc.Block("p2cl_%d" % jc) as blk:
                @blk.sync
                def _(sp, jc=jc, guard=guard):
                    self.waitbar(sp, "sp", guard)
                    self.inc("spd", sp.dma_start(
                        out=ek[:, :], in_=D["dr_ek"][:, jc, :]), 16)
                    self.inc("spd", sp.dma_start(
                        out=vt[:, :], in_=D["dr_v"][:, jc, :]), 16)
            ld = self.cnt["spd"]
            srt = self.T1[:, 0:EXT]
            self.mmchain(
                "p2r%d" % jc, [(jc, o, n) for (o, n) in CH_EXT], NJC,
                lambda jc_, ci: wr_b[:, ci, jc_ * 128:(jc_ + 1) * 128],
                lambda jc_, ci, o, n: self.xr[:, ci, o:o + n],
                lambda sc, pcol, jc_, o, n: sc.activation(
                    srt[:, o:o + n], pcol, AF.Sigmoid),
                consumer="s", pre_snap=guard, c_pre_snap=guard)
            sig_done = self.cnt["s"]

            with nc.Block("p2co_%d" % jc) as blk:
                @blk.vector
                def _(v, jc=jc, ld=ld, sig_done=sig_done):
                    self.w(v, "v", "spd", ld)
                    eku = self.T2[:, 0:EXT]
                    v.tensor_copy(P["lamc"][:, :], P["lam"][:, jc:jc + 1])
                    v.tensor_mul(nm[:, 0:TOK], ek[:, 0:TOK], vt[:, 0:TOK])
                    v.tensor_copy(ah[:, 0:1], P["initA"][:, jc:jc + 1])
                    v.tensor_copy(bh[:, 0:1], P["initB"][:, jc:jc + 1])
                    self.fence(v, "v")
                    lb = P["lamc"].broadcast_to([128, 1024])
                    self.inc("v", v.tensor_tensor_scan(
                        ah[:, 1:1025], lb, nm[:, 0:1024],
                        P["initA"][:, jc:jc + 1], ALU.mult, ALU.add))
                    self.inc("v", v.tensor_tensor_scan(
                        bh[:, 1:1025], lb, ek[:, 0:1024],
                        P["initB"][:, jc:jc + 1], ALU.mult, ALU.add))
                    self.fence(v, "v")
                    self.inc("v", v.tensor_tensor_scan(
                        ah[:, 1025:2049], lb, nm[:, 1024:2048],
                        ah[:, 1024:1025], ALU.mult, ALU.add))
                    self.inc("v", v.tensor_tensor_scan(
                        bh[:, 1025:2049], lb, ek[:, 1024:2048],
                        bh[:, 1024:1025], ALU.mult, ALU.add))
                    self.fence(v, "v")
                    v.tensor_scalar_mul(eku, ek[:, :],
                                        P["euk"][:, jc:jc + 1])
                    v.tensor_mul(nm[:, :], eku, vt[:, :])
                    v.tensor_add(nm[:, :], nm[:, :], ah[:, :])
                    v.scalar_tensor_tensor(bh[:, :], bh[:, :], 1e-12,
                                           eku, ALU.max, ALU.add)
                    v.reciprocal(bh[:, :], bh[:, :])
                    v.tensor_mul(nm[:, :], nm[:, :], bh[:, :])
                    self.w(v, "v", "s", sig_done)
                    self.inc("v", v.tensor_mul(self.rwkv[:, jc, :],
                                               nm[:, :], srt))
        pool.free("wr", "p2c_ek", "p2c_vt", "p2c_ah", "p2c_bh", "p2c_nm")

    def p3_ow(self):
        nc, D, P = self.nc, self.D, self.P
        pool = self.pool
        wo_b = pool.alloc("wo", [128, NJC, C], BF16)
        self.load_weight("wo", D["wo"], C, wo_b, pre_snap=self.bar())
        y1 = pool.alloc("y1", [128, NJC, EXT], F32)
        self.y1 = y1

        with nc.Block("p3_y0ld") as blk:
            @blk.sync
            def _(sp):
                ap = bass.AP(D["dr_y0"], 1,
                             [[NJC * NT, 128], [NT, NJC], [1, EXT]])
                self.inc("spd", sp.dma_start(out=y1[:], in_=ap), 16)
        wsnap = self.bar()

        items = [(jc, o, n) for jc in range(NJC) for (o, n) in CH_EXT]
        self.mmchain(
            "p3", items, NJC,
            lambda jc, ci: wo_b[:, ci, jc * 128:(jc + 1) * 128],
            lambda jc, ci, o, n: self.rwkv[:, ci, o:o + n],
            lambda v, pcol, jc, o, n: v.tensor_add(
                y1[:, jc, o:o + n], y1[:, jc, o:o + n], pcol),
            consumer="v", pre_snap=wsnap, c_pre_snap=wsnap)
        pool.free("wo", "rwkv")
        self.tap("y1", lambda: y1[:], [128, NJC, EXT], F32)

    def p4_ln2(self):
        nc, D, P = self.nc, self.D, self.P
        pool = self.pool
        y1 = self.y1
        xbf = pool.alloc("p4xbf", [128, NJC, EXT], BF16)
        sqbf = pool.alloc("p4sqbf", [128, NJC, EXT], BF16)
        mb, rb = self.MB, self.RB
        self.mk_sq("p4", lambda jc: y1[:, jc, :], xbf, sqbf, EXT,
                   pre_snap=self.bar())
        self.ln_stats("p4", EXT, xbf, sqbf, mb, rb, EPS_LN,
                      pre_snap=self.bar())
        stats_done = self.bar()
        pool.free("p4xbf", "p4sqbf")
        self.xk2 = pool.alloc("xk2", [128, NJC, TOK], BF16)
        self.xr2 = pool.alloc("xr2", [128, NJC, TOK], BF16)
        with nc.Block("p4_mix") as blk:
            st = self.bar()

            @blk.vector
            def _(v):
                self.waitbar(v, "v", st)
                ht = self.T2[:, 0:EXT]
                df = self.T3[:, 0:TOK]
                for jc in range(NJC):
                    v.tensor_sub(ht, y1[:, jc, :], mb[:, 0:EXT])
                    v.tensor_mul(ht, ht, rb[:, 0:EXT])
                    v.tensor_scalar(ht, ht, P["ln2w"][:, jc:jc + 1],
                                    P["ln2b"][:, jc:jc + 1], ALU.mult,
                                    ALU.add)
                    v.tensor_mul(self.T2[:, 0:1], self.T2[:, 0:1],
                                 P["bm"][:, 0:1])
                    self.fence(v, "v")
                    v.tensor_sub(df, self.T2[:, 1:EXT], self.T2[:, 0:EXT - 1])
                    v.scalar_tensor_tensor(self.xk2[:, jc, :], df,
                                           P["ftmk"][:, jc:jc + 1],
                                           self.T2[:, 0:EXT - 1],
                                           ALU.mult, ALU.add)
                    self.inc("v", v.scalar_tensor_tensor(
                        self.xr2[:, jc, :], df, P["ftmr"][:, jc:jc + 1],
                        self.T2[:, 0:EXT - 1], ALU.mult, ALU.add))
            mix_done = self.cnt["v"]

            @blk.sync
            def _(sp):
                self.w(sp, "sp", "v", mix_done)
                self.inc("spd", sp.dma_start(out=D["dr_y1"][:], in_=y1[:]),
                         16)
        self.y1_spilled = self.bar()

    def p5_cm(self):
        nc, D, P = self.nc, self.D, self.P
        pool = self.pool
        pool.free("y1")
        stg5 = pool.alloc("stg5", [128, NJC, 1024], F32)
        bfb = pool.alloc("bfb5", [128, NJC, 1024], BF16)
        snap0 = self.bar()
        for fo in range(4):
            with nc.Block("p5fkw%d" % fo) as blk:
                @blk.sync
                def _(sp, fo=fo):
                    self.waitbar(sp, "sp", snap0)
                    self.w(sp, "sp", "s", self.cnt.get("s", 0))
                    ap = bass.AP(D["fkw"], fo * 1024,
                                 [[DFF, 128], [128 * DFF, NJC], [1, 1024]])
                    self.inc("spd", sp.dma_start(out=stg5[:], in_=ap), 16)
                ldc = self.cnt["spd"]

                @blk.scalar
                def _(sc, fo=fo, ldc=ldc):
                    self.w(sc, "s", "spd", ldc)
                    self.w(sc, "s", "gd", self.cnt.get("gd", 0))
                    self.inc("s", sc.copy(bfb[:], stg5[:]))
                csc = self.cnt["s"]

                @blk.gpsimd
                def _(g, fo=fo, csc=csc):
                    self.w(g, "g", "s", csc)
                    for k in range(NJC):
                        out = bass.AP(D["dr_fkw"],
                                      (fo * 8 + k) * 128 * 1024,
                                      [[1024, 128], [128, NJC], [1, 128]])
                        self.inc("gd", g.dma_start(
                            out=out, in_=bfb[:, :, k * 128:(k + 1) * 128]),
                            16)
        for ro in range(4):
            with nc.Block("p5fvw%d" % ro) as blk:
                @blk.sync
                def _(sp, ro=ro):
                    self.w(sp, "sp", "s", self.cnt.get("s", 0))
                    ap = bass.AP(D["fvw"], ro * 1024 * 1024,
                                 [[1024, 128], [128 * 1024, NJC], [1, 1024]])
                    self.inc("spd", sp.dma_start(out=stg5[:], in_=ap), 16)
                ldc = self.cnt["spd"]

                @blk.scalar
                def _(sc, ro=ro, ldc=ldc):
                    self.w(sc, "s", "spd", ldc)
                    self.w(sc, "s", "gd", self.cnt.get("gd", 0))
                    self.inc("s", sc.copy(bfb[:], stg5[:]))
                csc = self.cnt["s"]

                @blk.gpsimd
                def _(g, ro=ro, csc=csc):
                    self.w(g, "g", "s", csc)
                    for k in range(NJC):
                        ft = ro * 8 + k
                        out = bass.AP(D["dr_fvw"], ft * 128,
                                      [[4096, 128], [128 * 4096, NJC],
                                       [1, 128]])
                        inap = bfb[:, k, :].rearrange("p (a b) -> p a b",
                                                      a=8)
                        self.inc("gd", g.dma_start(out=out, in_=inap), 16)
        stage_done = self.bar()
        pool.free("stg5", "bfb5")
        if self.stop == "p5s":
            return

        frw_b = pool.alloc("frw", [128, NJC, C], BF16)
        self.load_weight("frw", D["frw"], C, frw_b, pre_snap=stage_done)
        kk_b = pool.alloc("kk", [128, 32, 512], BF16)
        wbuf = [pool.alloc("p5wb%d" % i, [128, NJC, 128], BF16)
                for i in range(2)]
        fvb = [pool.alloc("p5fv%d" % i, [128, 32, 128], BF16)
               for i in range(2)]
        es5 = contextlib.ExitStack()
        psKK = [es5.enter_context(nc.psum_tensor("p5psKK%d" % i, [128, 512],
                                                 F32)) for i in range(2)]
        psKV = es5.enter_context(nc.psum_tensor("p5psKV", [128, 512], F32))
        psR2 = es5.enter_context(nc.psum_tensor("p5psR2", [128, 512], F32))
        wsnap = self.bar()

        for tq in range(4):
            q0 = tq * 512
            with nc.Block("p5kk_%d" % tq) as blk:
                spdb = self.cnt["spd"]
                tb = self.cnt["t"]
                sb_ = self.cnt["s"]
                vb = self.cnt["v"]

                @blk.sync
                def _(sp, tq=tq, tb=tb):
                    self.waitbar(sp, "sp", wsnap)
                    for ft in range(32):
                        if ft >= 2:
                            self.w(sp, "sp", "t", tb + ft - 1)
                        inap = bass.AP(D["dr_fkw"], ft * 128 * 1024,
                                       [[1024, 128], [128, NJC], [1, 128]])
                        self.inc("spd", sp.dma_start(out=wbuf[ft % 2][:],
                                                     in_=inap), 16)

                @blk.tensor
                def _(te, tq=tq, q0=q0, spdb=spdb, vb=vb):
                    self.waitbar(te, "t", wsnap)
                    for ft in range(32):
                        self.w(te, "t", "spd", spdb + (ft + 1) * 16)
                        if ft >= 2:
                            self.w(te, "t", "v", vb + ft - 1)
                        for ci in range(NJC):
                            ins = te.matmul(psKK[ft % 2][:],
                                            wbuf[ft % 2][:, ci, :],
                                            self.xk2[:, ci, q0:q0 + 512],
                                            start=(ci == 0),
                                            stop=(ci == NJC - 1))
                        self.inc("t", ins)

                @blk.scalar
                def _(sc, tq=tq, tb=tb, vb=vb):
                    self.w(sc, "s", "v", vb)
                    for ft in range(32):
                        self.w(sc, "s", "t", tb + ft + 1)
                        if ft >= 1:
                            self.w(sc, "s", "v", vb + ft)
                        self.inc("s", sc.activation(self.T1[:, 0:512],
                                                    psKK[ft % 2][:],
                                                    AF.Relu))

                @blk.vector
                def _(v, tq=tq, sb_=sb_):
                    for ft in range(32):
                        self.w(v, "v", "s", sb_ + ft + 1)
                        self.inc("v", v.tensor_mul(kk_b[:, ft, :],
                                                   psKK[ft % 2][:],
                                                   self.T1[:, 0:512]))

            if self.stop == "p5k" and tq == 1:
                break
            if self.stop == "p5kk0":
                break
            with nc.Block("p5kv_%d" % tq) as blk:
                spdb = self.cnt["spd"]
                tb = self.cnt["t"]
                sb_ = self.cnt["s"]
                vb = self.cnt["v"]
                kvsnap = self.bar()

                def spd_after(jc, spdb=spdb):
                    return spdb + 32 + jc * 48

                @blk.sync
                def _(sp, tq=tq, q0=q0, tb=tb, vb=vb):
                    self.w(sp, "sp", "v", vb)
                    for jc in range(NJC):
                        if jc >= 1:
                            self.w(sp, "sp", "v", vb + jc)
                            self.inc("spd", sp.dma_start(
                                out=D["dr_y2"][:, jc - 1, q0:q0 + 512],
                                in_=self.T3[:, 0:512]), 16)
                        if jc >= 2:
                            self.w(sp, "sp", "t", tb + (jc - 1) * 2)
                        inap = bass.AP(D["dr_fvw"], jc * 128 * 4096,
                                       [[4096, 128], [128, 32], [1, 128]])
                        self.inc("spd", sp.dma_start(out=fvb[jc % 2][:],
                                                     in_=inap), 16)
                        self.inc("spd", sp.dma_start(
                            out=self.T3[:, 0:512],
                            in_=bass.AP(D["dr_y1"],
                                        jc * EXT + 1 + q0,
                                        [[NJC * EXT, 128], [1, 512]])), 16)
                    self.w(sp, "sp", "v", vb + NJC)
                    self.inc("spd", sp.dma_start(
                        out=D["dr_y2"][:, NJC - 1, q0:q0 + 512],
                        in_=self.T3[:, 0:512]), 16)

                @blk.tensor
                def _(te, tq=tq, q0=q0, vb=vb, sb_=sb_, kvsnap=kvsnap):
                    self.waitbar(te, "t", kvsnap)
                    for jc in range(NJC):
                        self.w(te, "t", "spd", spd_after(jc))
                        if jc >= 1:
                            self.w(te, "t", "v", vb + jc)
                            self.w(te, "t", "s", sb_ + jc)
                        for ft in range(32):
                            ins = te.matmul(psKV[:], fvb[jc % 2][:, ft, :],
                                            kk_b[:, ft, :],
                                            start=(ft == 0), stop=(ft == 31))
                        self.inc("t", ins)
                        for ci in range(NJC):
                            ins = te.matmul(
                                psR2[:],
                                frw_b[:, ci, jc * 128:(jc + 1) * 128],
                                self.xr2[:, ci, q0:q0 + 512],
                                start=(ci == 0), stop=(ci == NJC - 1))
                        self.inc("t", ins)

                @blk.scalar
                def _(sc, tq=tq, tb=tb, vb=vb):
                    self.w(sc, "s", "v", vb)
                    for jc in range(NJC):
                        self.w(sc, "s", "t", tb + jc * 2 + 2)
                        if jc >= 1:
                            self.w(sc, "s", "v", vb + jc)
                        self.inc("s", sc.activation(self.T2[:, 0:512],
                                                    psR2[:], AF.Sigmoid))

                @blk.vector
                def _(v, tq=tq, q0=q0, tb=tb, sb_=sb_):
                    for jc in range(NJC):
                        self.w(v, "v", "t", tb + jc * 2 + 1)
                        self.w(v, "v", "s", sb_ + jc + 1)
                        self.w(v, "v", "spd", spd_after(jc))
                        v.tensor_mul(self.T1[:, 0:512], psKV[:],
                                     self.T2[:, 0:512])
                        self.inc("v", v.tensor_add(self.T3[:, 0:512],
                                                   self.T3[:, 0:512],
                                                   self.T1[:, 0:512]))
        if self.stop in ("p5k", "p5kk0"):
            es5.close()
            return
        es5.close()
        pool.free("frw", "kk", "p5wb0", "p5wb1", "p5fv0", "p5fv1",
                  "xk2", "xr2")
        if "y2" in self.taps:
            h = self.nc.dram_tensor("dbg_y2", [128, NJC, TOK], F32,
                                    kind="ExternalOutput")
            with nc.Block("tap_y2") as blk:
                @blk.sync
                def _(sp):
                    self.waitbar(sp, "sp")
                    self.inc("spd", sp.dma_start(out=h[:],
                                                 in_=D["dr_y2"][:]), 16)


    def p6_x1(self):
        nc, D, P = self.nc, self.D, self.P
        pool = self.pool
        xbf = pool.alloc("p6xbf", [128, NJC, TOK], BF16)
        sqbf = pool.alloc("p6sqbf", [128, NJC, TOK], BF16)
        mb, rb = self.MB, self.RB
        self.mk_sq_dram("p6", lambda jc: D["dr_y2"][:, jc, :], xbf, sqbf,
                        TOK, pre_snap=self.bar())
        self.ln_stats("p6", TOK, xbf, sqbf, mb, rb, EPS_LN,
                      pre_snap=self.bar())
        sdone = self.bar()
        pool.free("p6xbf", "p6sqbf")
        x1 = pool.alloc("x1", [128, NJC, TOK], F32)
        self.x1 = x1
        with nc.Block("p6x") as blk:
            lds = []
            vb0 = self.cnt["v"]

            @blk.sync
            def _(sp):
                self.waitbar(sp, "sp", sdone)
                for jc in range(NJC):
                    if jc >= 1:
                        self.w(sp, "sp", "v", vb0 + jc)
                    lds.append(self.inc("spd", sp.dma_start(
                        out=self.T1[:, 0:TOK], in_=D["dr_y2"][:, jc, :]),
                        16))
                    lds.append(self.inc("spd", sp.dma_start(
                        out=self.T2[:, 0:TOK], in_=D["x"][:, jc, 2:NT]), 16))

            @blk.vector
            def _(v):
                self.waitbar(v, "v", sdone)
                for jc in range(NJC):
                    self.w(v, "v", "spd", lds[2 * jc + 1])
                    v.tensor_sub(x1[:, jc, :], self.T1[:, 0:TOK],
                                 mb[:, 0:TOK])
                    v.tensor_mul(x1[:, jc, :], x1[:, jc, :], rb[:, 0:TOK])
                    self.inc("v", v.tensor_add(x1[:, jc, :], x1[:, jc, :],
                                               self.T2[:, 0:TOK]))
        self.tap("x1", lambda: x1[:], [128, NJC, TOK], F32)

    def p7_lnB(self):
        nc, D, P = self.nc, self.D, self.P
        pool = self.pool
        x1 = self.x1
        xbf = pool.alloc("p7xbf", [128, NJC, TOK], BF16)
        sqbf = pool.alloc("p7sqbf", [128, NJC, TOK], BF16)
        mb, rb = self.MB, self.RB
        self.mk_sq("p7", lambda jc: x1[:, jc, :], xbf, sqbf, TOK,
                   pre_snap=self.bar())
        self.ln_stats("p7", TOK, xbf, sqbf, mb, rb, EPS_LN,
                      pre_snap=self.bar())
        with nc.Block("p7_y0b") as blk:
            st = self.bar()

            @blk.sync
            def _(sp):
                self.waitbar(sp, "sp", st)
                self.inc("spd", sp.dma_start(out=D["dr_x1"][:], in_=x1[:]),
                         16)
            x1_sp = self.cnt["spd"]

            @blk.vector
            def _(v):
                self.waitbar(v, "v", st)
                self.w(v, "v", "spd", x1_sp)
                for jc in range(NJC):
                    v.tensor_sub(self.T1[:, 0:TOK], x1[:, jc, :],
                                 mb[:, 0:TOK])
                    v.tensor_mul(x1[:, jc, :], self.T1[:, 0:TOK],
                                 rb[:, 0:TOK])
                    v.tensor_copy(xbf[:, jc, :], x1[:, jc, :])
                    self.inc("v", v.tensor_mul(sqbf[:, jc, :],
                                               xbf[:, jc, :], xbf[:, jc, :]))
            yb_done = self.cnt["v"]

            @blk.sync
            def _(sp):
                self.w(sp, "sp", "v", yb_done)
                self.inc("spd", sp.dma_start(out=D["dr_y0b"][:], in_=x1[:]),
                         16)
        self.tap("y0b", lambda: x1[:], [128, NJC, TOK], F32)
        self.ln_stats("p7r", TOK, None, sqbf, None, rb, EPS_RMS,
                      subtract_mean=False, pre_snap=self.bar())
        sdone = self.bar()
        pool.free("p7xbf", "p7sqbf")
        nx = pool.alloc("nx", [128, NJC, TOK], BF16)
        self.nx = nx
        with nc.Block("p7_nx") as blk:
            st = self.bar()

            @blk.vector
            def _(v):
                self.waitbar(v, "v", st)
                for jc in range(NJC):
                    v.tensor_mul(self.T1[:, 0:TOK], x1[:, jc, :],
                                 rb[:, 0:TOK])
                    self.inc("v", v.tensor_scalar_mul(
                        nx[:, jc, :], self.T1[:, 0:TOK],
                        P["g1"][:, jc:jc + 1]))
        pool.free("x1")
        self.tap("nx", lambda: nx[:], [128, NJC, TOK], BF16)

    def p8_qkv(self):
        nc, D, P = self.nc, self.D, self.P
        pool = self.pool
        q_b = pool.alloc("q_b", [128, NJC, TOK], BF16)
        wq_b = pool.alloc("wq", [128, NJC, C], BF16)
        wkc_b = pool.alloc("wkc", [128, NJC, DK], BF16)
        wvc_b = pool.alloc("wvc", [128, NJC, DK], BF16)
        self.load_weight("wq", D["wq"], C, wq_b, pre_snap=self.bar())
        self.load_weight("wkc", D["wkc"], DK, wkc_b)
        self.load_weight("wvc", D["wvc"], DK, wvc_b)
        kc_loc = pool.alloc("kc_loc", [128, 2, TOK], BF16)
        vc_loc = pool.alloc("vc_loc", [128, 16, DK], BF16)
        self.q_b = q_b
        wsnap = self.bar()
        nx = self.nx

        self.mmchain(
            "p8q", [(dt, o, n) for dt in range(NJC) for (o, n) in CH_TOK],
            NJC,
            lambda dt, ci: wq_b[:, ci, dt * 128:(dt + 1) * 128],
            lambda dt, ci, o, n: nx[:, ci, o:o + n],
            lambda v, pcol, dt, o, n: v.tensor_scalar_mul(
                q_b[:, dt, o:o + n], pcol, 0.0625),
            consumer="v", pre_snap=wsnap, c_pre_snap=wsnap)
        self.mmchain(
            "p8kc", [(dj, o, n) for dj in range(2) for (o, n) in CH_TOK],
            NJC,
            lambda dj, ci: wkc_b[:, ci, dj * 128:(dj + 1) * 128],
            lambda dj, ci, o, n: nx[:, ci, o:o + n],
            lambda v, pcol, dj, o, n: v.tensor_copy(kc_loc[:, dj, o:o + n],
                                                    pcol),
            consumer="v")
        self.mmchain(
            "p8vc", [(tt, 0, DK) for tt in range(16)], NJC,
            lambda tt, ci: nx[:, ci, tt * 128:(tt + 1) * 128],
            lambda tt, ci, o, n: wvc_b[:, ci, :],
            lambda v, pcol, tt, o, n: v.tensor_copy(vc_loc[:, tt, :], pcol),
            consumer="v", psum_shape=(128, DK))

        NAT = self.NAT
        snap = self.bar()
        with nc.Block("p8_agout") as blk:
            @blk.gpsimd
            def _(g):
                self.waitbar(g, "g", snap)
                for dj in range(2):
                    self.inc("gd", g.dma_start(
                        out=bass.AP(D["cc_at_in"], dj * 128 * TOK,
                                    [[TOK, 128], [1, TOK]]),
                        in_=kc_loc[:, dj, :]), 16)
                self.inc("gd", g.dma_start(
                    out=bass.AP(D["cc_at_in"], 2 * 128 * TOK,
                                [[DK, 128], [128 * DK, 16], [1, DK]]),
                    in_=vc_loc[:, :, :]), 16)
                self.w(g, "g", "gd", self.cnt["gd"])
                self.inc("cc", g.collective_compute(
                    "AllGather", ALU.bypass,
                    replica_groups=[[0, 1, 2, 3], [4, 5, 6, 7]],
                    ins=[D["cc_at_in"][:]],
                    outs=[D["cc_at_out"][:]]))
        agq = self.bar()
        pool.free("wq", "wkc", "wvc", "nx", "kc_loc", "vc_loc")
        kcg = pool.alloc("kcg", [128, 2, 8192], BF16)
        vcg = pool.alloc("vcg", [128, 64, DK], BF16)
        self.att_b = pool.alloc("att_b", [128, NJC, TOK], BF16)
        self.kcg, self.vcg = kcg, vcg
        with nc.Block("p8_agin") as blk:
            @blk.gpsimd
            def _(g):
                self.w(g, "g", "cc", self.cnt["cc"])
                for r in range(4):
                    for dj in range(2):
                        self.inc("gd", g.dma_start(
                            out=kcg[:, dj, r * TOK:(r + 1) * TOK],
                            in_=bass.AP(D["cc_at_out"],
                                        r * NAT + dj * 128 * TOK,
                                        [[TOK, 128], [1, TOK]])), 16)
                    self.inc("gd", g.dma_start(
                        out=vcg[:, r * 16:(r + 1) * 16, :],
                        in_=bass.AP(D["cc_at_out"], r * NAT + 2 * 128 * TOK,
                                    [[DK, 128], [128 * DK, 16], [1, DK]])),
                        16)
        self.tap("kcg", lambda: kcg[:], [128, 2, 8192], BF16)
        self.tap("vcg", lambda: vcg[:], [128, 64, DK], BF16)

    def p9_attn(self):
        nc, D, P = self.nc, self.D, self.P
        pool = self.pool
        q_b, kcg, vcg = self.q_b, self.kcg, self.vcg
        att_b = self.att_b
        nm_sl = pool.alloc("nm_sl", [128, 16, 8], F32)
        ssl = pool.alloc("ssl", [128, 16, 8], F32)
        earg = pool.alloc("earg", [128, 16, 8], F32)
        Mneg = pool.alloc("Mneg", [128, 16], F32)
        Zt = pool.alloc("Zt", [128, 16], F32)
        lnZ = pool.alloc("lnZ", [128, 16], F32)
        btile = pool.alloc("btile", [128, 16], F32)
        scr = pool.alloc("scr", [128, 1024], BF16)
        pt_b = [pool.alloc("pt%d" % i, [128, 1024], BF16) for i in range(2)]
        es9 = contextlib.ExitStack()
        psS = [es9.enter_context(nc.psum_tensor("p9psS%d" % i, [128, 1024],
                                                F32)) for i in range(2)]
        psP = [es9.enter_context(nc.psum_tensor("p9psP%d" % i, [128, 1024],
                                                F32)) for i in range(2)]
        gsnap = self.bar()

        for h in range(NHEAD):
            with nc.Block("p9a_%d" % h) as blk:
                tb = self.cnt["t"]
                vb = self.cnt["v"]
                sb_ = self.cnt["s"]

                @blk.tensor
                def _(te, h=h, vb=vb, sb_=sb_):
                    self.waitbar(te, "t", gsnap)
                    self.waitbar(te, "t")
                    for qt in range(16):
                        for tg in range(8):
                            idx = qt * 8 + tg
                            if idx >= 2:
                                self.w(te, "t", "s", sb_ + idx - 1)
                            for cc in range(2):
                                for dj in range(2):
                                    ins = te.matmul(
                                        psS[idx % 2][:, cc * 512:
                                                     (cc + 1) * 512],
                                        q_b[:, 2 * h + dj,
                                            qt * 128:(qt + 1) * 128],
                                        kcg[:, dj, tg * 1024 + cc * 512:
                                            tg * 1024 + (cc + 1) * 512],
                                        start=(dj == 0), stop=(dj == 1))
                            self.inc("t", ins)

                @blk.vector
                def _(v, h=h, tb=tb):
                    for qt in range(16):
                        for tg in range(8):
                            idx = qt * 8 + tg
                            self.w(v, "v", "t", tb + idx + 1)
                            self.inc("v", v.reduce_max(
                                nm_sl[:, qt, tg:tg + 1],
                                psS[idx % 2][:, :], axis=AX.X, negate=True))

                @blk.scalar
                def _(sc, h=h, vb=vb):
                    for qt in range(16):
                        for tg in range(8):
                            idx = qt * 8 + tg
                            self.w(sc, "s", "v", vb + idx + 1)
                            self.inc("s", sc.activation(
                                scr[:, :], psS[idx % 2][:, :], AF.Exp,
                                bias=nm_sl[:, qt, tg:tg + 1],
                                accum_out=ssl[:, qt, tg:tg + 1]))

            with nc.Block("p9b_%d" % h) as blk:
                ps_ = self.cnt["s"]

                @blk.vector
                def _(v, ps_=ps_):
                    self.w(v, "v", "s", ps_)
                    self.inc("v", v.tensor_reduce(
                        Mneg[:, :], nm_sl[:, :, :], AX.X, ALU.min))
                    self.fence(v, "v")
                    for qt in range(16):
                        ins = v.tensor_scalar(earg[:, qt, :],
                                              nm_sl[:, qt, :], -1.0,
                                              Mneg[:, qt:qt + 1],
                                              ALU.mult, ALU.add)
                    self.inc("v", ins)
                v1 = self.cnt["v"]

                @blk.scalar
                def _(sc, v1=v1):
                    self.w(sc, "s", "v", v1)
                    self.inc("s", sc.activation(earg[:, :, :],
                                                earg[:, :, :], AF.Exp))
                s1 = self.cnt["s"]

                @blk.vector
                def _(v, s1=s1):
                    self.w(v, "v", "s", s1)
                    v.tensor_mul(earg[:, :, :], earg[:, :, :], ssl[:, :, :])
                    self.fence(v, "v")
                    self.inc("v", v.tensor_reduce(
                        Zt[:, :], earg[:, :, :], AX.X, ALU.add))
                v2 = self.cnt["v"]

                @blk.scalar
                def _(sc, v2=v2):
                    self.w(sc, "s", "v", v2)
                    self.inc("s", sc.activation(lnZ[:, :], Zt[:, :], AF.Ln))
                s2 = self.cnt["s"]

                @blk.vector
                def _(v, s2=s2):
                    self.w(v, "v", "s", s2)
                    self.inc("v", v.tensor_sub(btile[:, :], Mneg[:, :],
                                               lnZ[:, :]))
                v3 = self.cnt["v"]

                @blk.tensor
                def _(te, v3=v3):
                    self.w(te, "t", "v", v3)
                    self.inc("t", te.transpose(psS[0][0:16, 0:128],
                                               btile[:, :], P["ident"][:]))
                t1 = self.cnt["t"]

                @blk.vector
                def _(v, t1=t1):
                    self.w(v, "v", "t", t1)
                    self.inc("v", v.tensor_copy(self.T2[0:16, 0:128],
                                                psS[0][0:16, 0:128]))
                v4 = self.cnt["v"]

                @blk.sync
                def _(sp, v4=v4):
                    self.w(sp, "sp", "v", v4)
                    self.inc("spd", sp.dma_start(
                        out=bass.AP(D["dr_brow"], 0, [[128, 16], [1, 128]]),
                        in_=self.T2[0:16, 0:128]), 16)
                    self.w(sp, "sp", "spd", self.cnt["spd"])
                    self.inc("spd", sp.dma_start(
                        out=self.T3[0:1, 0:TOK],
                        in_=bass.AP(D["dr_brow"], 0,
                                    [[2048, 1], [1, 2048]])), 16)
            brow_ld = self.cnt["spd"]
            brow = self.T3[0:1, 0:TOK]

            for qh in range(2):
                with nc.Block("p9c_%d_%d" % (h, qh)) as blk:
                    tb = self.cnt["t"]
                    sb_ = self.cnt["s"]
                    vprev = self.cnt["v"]

                    @blk.tensor
                    def _(te, h=h, qh=qh, sb_=sb_, brow_ld=brow_ld,
                          vprev=vprev):
                        self.w(te, "t", "spd", brow_ld)
                        self.w(te, "t", "v", vprev)
                        for tt in range(64):
                            if tt >= 2:
                                self.w(te, "t", "s", sb_ + tt - 1)
                            for cc in range(2):
                                for dj in range(2):
                                    te.matmul(
                                        psS[tt % 2][:, cc * 512:
                                                    (cc + 1) * 512],
                                        kcg[:, dj, tt * 128:(tt + 1) * 128],
                                        q_b[:, 2 * h + dj,
                                            qh * 1024 + cc * 512:
                                            qh * 1024 + (cc + 1) * 512],
                                        start=(dj == 0), stop=False)
                                ins = te.matmul(
                                    psS[tt % 2][:, cc * 512:(cc + 1) * 512],
                                    P["onesrow_f"][:],
                                    brow[:, qh * 1024 + cc * 512:
                                         qh * 1024 + (cc + 1) * 512],
                                    start=False, stop=True)
                            self.inc("t", ins)
                            self.w(te, "t", "s", sb_ + tt + 1)
                            for dj in range(2):
                                for cc in range(2):
                                    ins = te.matmul(
                                        psP[dj][:, cc * 512:(cc + 1) * 512],
                                        vcg[:, tt, dj * 128:(dj + 1) * 128],
                                        pt_b[tt % 2][:, cc * 512:
                                                     (cc + 1) * 512],
                                        start=(tt == 0), stop=(tt == 63),
                                        skip_group_check=True)
                            self.inc("t", ins)

                    @blk.scalar
                    def _(sc, h=h, qh=qh, tb=tb):
                        for tt in range(64):
                            self.w(sc, "s", "t", tb + tt * 2 + 1)
                            self.inc("s", sc.activation(
                                pt_b[tt % 2][:, :], psS[tt % 2][:, :],
                                AF.Exp))
                    tlast = self.cnt["t"]

                    @blk.vector
                    def _(v, h=h, qh=qh, tlast=tlast):
                        self.w(v, "v", "t", tlast)
                        for dj in range(2):
                            self.inc("v", v.tensor_copy(
                                att_b[:, 2 * h + dj,
                                      qh * 1024:(qh + 1) * 1024],
                                psP[dj][:, :]))
        es9.close()
        pool.free("q_b", "kcg", "vcg", "nm_sl", "ssl", "earg", "Mneg",
                  "Zt", "lnZ", "btile", "scr", "pt0", "pt1")
        self.tap("attb", lambda: att_b[:], [128, NJC, TOK], BF16)

    def p10_wo(self):
        nc, D, P = self.nc, self.D, self.P
        pool = self.pool
        wwo_b = pool.alloc("wwo", [128, NJC, C], BF16)
        self.load_weight("wwo", D["wwo"], C, wwo_b, pre_snap=self.bar())
        Xb = pool.alloc("Xb", [128, NJC, TOK], F32)
        self.Xb = Xb
        with nc.Block("p10_ld") as blk:
            @blk.sync
            def _(sp):
                self.inc("spd", sp.dma_start(out=Xb[:], in_=D["dr_y0b"][:]),
                         16)
        wsnap = self.bar()
        self.mmchain(
            "p10", [(jc, o, n) for jc in range(NJC) for (o, n) in CH_TOK],
            NJC,
            lambda jc, ci: wwo_b[:, ci, jc * 128:(jc + 1) * 128],
            lambda jc, ci, o, n: self.att_b[:, ci, o:o + n],
            lambda v, pcol, jc, o, n: v.tensor_add(
                Xb[:, jc, o:o + n], Xb[:, jc, o:o + n], pcol),
            consumer="v", pre_snap=wsnap, c_pre_snap=wsnap)
        mm_done = self.bar()
        pool.free("wwo", "att_b")
        self.tap("xbres", lambda: Xb[:], [128, NJC, TOK], F32)

        xbf = pool.alloc("p10xbf", [128, NJC, TOK], BF16)
        sqbf = pool.alloc("p10sqbf", [128, NJC, TOK], BF16)
        rb = self.RB
        self.mk_sq("p10", lambda jc: Xb[:, jc, :], xbf, sqbf, TOK,
                   pre_snap=self.bar())
        self.ln_stats("p10r", TOK, None, sqbf, None, rb, EPS_RMS,
                      subtract_mean=False, pre_snap=self.bar())
        sdone = self.bar()
        pool.free("p10xbf", "p10sqbf")
        nx2 = pool.alloc("nx2", [128, NJC, TOK], BF16)
        self.nx2 = nx2
        with nc.Block("p10_nx2") as blk:
            st = self.bar()

            @blk.vector
            def _(v):
                self.waitbar(v, "v", st)
                for jc in range(NJC):
                    v.tensor_mul(self.T1[:, 0:TOK], Xb[:, jc, :],
                                 rb[:, 0:TOK])
                    self.inc("v", v.tensor_scalar_mul(
                        nx2[:, jc, :], self.T1[:, 0:TOK],
                        P["g2"][:, jc:jc + 1]))
            nx2_done = self.cnt["v"]

            @blk.sync
            def _(sp):
                self.w(sp, "sp", "v", nx2_done)
                self.inc("spd", sp.dma_start(out=D["dr_xb"][:], in_=Xb[:]),
                         16)
        xb_spilled = self.bar()
        pool.free("Xb")
        self.tap("nx2", lambda: nx2[:], [128, NJC, TOK], BF16)

    def p11_router(self):
        nc, D, P = self.nc, self.D, self.P
        pool = self.pool
        wwr_b = pool.alloc("wwr", [128, NJC, 8], BF16)
        self.load_weight("wwr", D["wwr"], 8, wwr_b, pre_snap=self.bar())
        lg = pool.alloc("lg", [128, 16, 8], F32)
        ex = pool.alloc("ex", [128, 16, 8], F32)
        rw = pool.alloc("rw", [128, 16, 8], F32)
        eq1 = pool.alloc("eq1", [128, 16, 8], F32)
        eq2 = pool.alloc("eq2", [128, 16, 8], F32)
        comb = pool.alloc("comb", [128, 16, 8], F32)
        nmx = pool.alloc("nmx", [128, 16], F32)
        zs = pool.alloc("zs", [128, 16], F32)
        m1 = pool.alloc("m1", [128, 16], F32)
        m2 = pool.alloc("m2", [128, 16], F32)
        w1 = pool.alloc("w1", [128, 16], F32)
        w2 = pool.alloc("w2", [128, 16], F32)
        combT = pool.alloc("combT", [8, 16, 128], F32)
        es11 = contextlib.ExitStack()
        psL = [es11.enter_context(nc.psum_tensor("p11psL%d" % i, [128, 8],
                                                 F32)) for i in range(2)]
        psTr = [es11.enter_context(nc.psum_tensor("p11psT%d" % i,
                                                  [128, 128], F32))
                for i in range(2)]
        wsnap = self.bar()
        nx2 = self.nx2

        with nc.Block("p11_lg") as blk:
            tb = self.cnt["t"]
            vb = self.cnt["v"]

            @blk.tensor
            def _(te, vb=vb):
                self.waitbar(te, "t", wsnap)
                for tt in range(16):
                    if tt >= 2:
                        self.w(te, "t", "v", vb + tt - 1)
                    for ci in range(NJC):
                        te.matmul(psL[tt % 2][:],
                                  nx2[:, ci, tt * 128:(tt + 1) * 128],
                                  wwr_b[:, ci, :],
                                  start=(ci == 0), stop=False)
                    ins = te.matmul(psL[tt % 2][:], P["onesrow_f"][:],
                                    P["ebias"][:, :], start=False, stop=True)
                    self.inc("t", ins)

            @blk.vector
            def _(v, tb=tb):
                for tt in range(16):
                    self.w(v, "v", "t", tb + tt + 1)
                    self.inc("v", v.tensor_copy(lg[:, tt, :],
                                                psL[tt % 2][:]))

        with nc.Block("p11_sm") as blk:
            vp = self.cnt["v"]

            @blk.vector
            def _(v, vp=vp):
                self.w(v, "v", "v", vp)
                self.inc("v", v.tensor_reduce(nmx[:, :], lg[:, :, :], AX.X,
                                              ALU.max, negate=True))
            v1 = self.cnt["v"]

            @blk.scalar
            def _(sc, v1=v1):
                self.w(sc, "s", "v", v1)
                for tt in range(16):
                    self.inc("s", sc.activation(
                        ex[:, tt, :], lg[:, tt, :], AF.Exp,
                        bias=nmx[:, tt:tt + 1],
                        accum_out=zs[:, tt:tt + 1]))
            s1 = self.cnt["s"]

            @blk.vector
            def _(v, s1=s1):
                self.w(v, "v", "s", s1)
                self.inc("v", v.reciprocal(zs[:, :], zs[:, :]))
                self.fence(v, "v")
                for tt in range(16):
                    v.tensor_scalar_mul(rw[:, tt, :], ex[:, tt, :],
                                        zs[:, tt:tt + 1])
                self.fence(v, "v")
                self.inc("v", v.tensor_reduce(m1[:, :], rw[:, :, :], AX.X,
                                              ALU.max))
                self.fence(v, "v")
                for tt in range(16):
                    v.tensor_single_scalar(eq1[:, tt, :], rw[:, tt, :],
                                           m1[:, tt:tt + 1], ALU.is_equal)
                self.fence(v, "v")
                v.scalar_tensor_tensor(ex[:, :, :], eq1[:, :, :], -1e30,
                                       rw[:, :, :], ALU.mult, ALU.add)
                self.fence(v, "v")
                self.inc("v", v.tensor_reduce(m2[:, :], ex[:, :, :], AX.X,
                                              ALU.max))
                self.fence(v, "v")
                for tt in range(16):
                    v.tensor_single_scalar(eq2[:, tt, :], ex[:, tt, :],
                                           m2[:, tt:tt + 1], ALU.is_equal)
                self.fence(v, "v")
                self.inc("v", v.engine_nop())
            v2 = self.cnt["v"]

            @blk.scalar
            def _(sc, v2=v2):
                self.w(sc, "s", "v", v2)
                sc.activation(w1[:, :], m1[:, :], AF.Exp)
                self.inc("s", sc.activation(w2[:, :], m2[:, :], AF.Exp))
            s2 = self.cnt["s"]

            @blk.vector
            def _(v, s2=s2):
                self.w(v, "v", "s", s2)
                v.tensor_add(m1[:, :], w1[:, :], w2[:, :])
                self.fence(v, "v")
                v.reciprocal(m2[:, :], m1[:, :])
                self.fence(v, "v")
                v.tensor_mul(w1[:, :], w1[:, :], m2[:, :])
                v.tensor_mul(w2[:, :], w2[:, :], m2[:, :])
                self.fence(v, "v")
                for tt in range(16):
                    v.tensor_scalar_mul(comb[:, tt, :], eq1[:, tt, :],
                                        w1[:, tt:tt + 1])
                self.fence(v, "v")
                for tt in range(16):
                    v.scalar_tensor_tensor(comb[:, tt, :], eq2[:, tt, :],
                                           w2[:, tt:tt + 1], comb[:, tt, :],
                                           ALU.mult, ALU.add)
                self.inc("v", v.engine_nop())
            v3 = self.cnt["v"]

        with nc.Block("p11_tr") as blk:
            tb = self.cnt["t"]
            vb = self.cnt["v"]

            @blk.tensor
            def _(te, v3=v3, vb=vb):
                self.w(te, "t", "v", v3)
                for tt in range(16):
                    if tt >= 2:
                        self.w(te, "t", "v", vb + tt - 1)
                    self.inc("t", te.transpose(psTr[tt % 2][0:8, :],
                                               comb[:, tt, :],
                                               P["ident"][:]))

            @blk.vector
            def _(v, tb=tb):
                for tt in range(16):
                    self.w(v, "v", "t", tb + tt + 1)
                    self.inc("v", v.tensor_copy(combT[:, tt, :],
                                                psTr[tt % 2][0:8, :]))
            vc = self.cnt["v"]

            @blk.sync
            def _(sp, vc=vc):
                self.w(sp, "sp", "v", vc)
                self.inc("spd", sp.dma_start(
                    out=bass.AP(D["dr_comb"], 0,
                                [[TOK, 8], [128, 16], [1, 128]]),
                    in_=combT[:, :, :]), 16)
        es11.close()
        pool.free("wwr", "lg", "ex", "rw", "eq1", "eq2", "comb", "nmx",
                  "zs", "m1", "m2", "w1", "w2", "combT")
        self.tap("comb", lambda: None, [8, TOK], F32) if False else None

    def p12_moe(self):
        nc, D, P = self.nc, self.D, self.P
        pool = self.pool
        spec = pool.alloc("spec", [128, NJC, TOK], F32)
        self.spec = spec
        hh = pool.alloc("hh", [128, NJC, 1024], BF16)
        w1b = [pool.alloc("w1b%d" % i, [128, NJC, 128], BF16)
               for i in range(2)]
        w3b = [pool.alloc("w3b%d" % i, [128, NJC, 128], BF16)
               for i in range(2)]
        w2b = [pool.alloc("w2b%d" % i, [128, NJC, 128], BF16)
               for i in range(2)]
        stgt = pool.alloc("stgt", [128, NJC, 128], F32)
        tmpo = pool.alloc("tmpo", [128, 1024], F32)
        es12 = contextlib.ExitStack()
        psH1 = es12.enter_context(nc.psum_tensor("p12psH1", [128, 1024],
                                                 F32))
        psH3 = es12.enter_context(nc.psum_tensor("p12psH3", [128, 1024],
                                                 F32))
        psO = [es12.enter_context(nc.psum_tensor("p12psO", [128, 1024],
                                                 F32))] * 2
        psCb = es12.enter_context(nc.psum_tensor("p12psCb", [128, 512],
                                                 F32))
        nx2 = self.nx2
        cbb = self.T3[:, 0:TOK]
        cbrow = self.T2[0:1, 0:TOK]

        experts = [("sh", D["s1"], D["s3"], D["s2"], 0)] + [
            ("e%d" % e, D["e1"], D["e3"], D["e2"], e * C)
            for e in range(NEXP)]
        for ei, (tag, d1, d3, d2, roff) in enumerate(experts):
            if tag != "sh":
                with nc.Block("p12cb_%s" % tag) as blk:
                    vp = self.cnt["v"]
                    cbsnap = self.bar()

                    @blk.sync
                    def _(sp, ei=ei, vp=vp):
                        self.waitbar(sp, "sp")
                        self.inc("spd", sp.dma_start(
                            out=cbrow, in_=D["dr_comb"][ei - 1]), 16)
                    ld = self.cnt["spd"]
                    tb = self.cnt["t"]
                    vb_cb = self.cnt["v"]

                    @blk.tensor
                    def _(te, ld=ld, vb_cb=vb_cb):
                        self.w(te, "t", "spd", ld)
                        for cc in range(4):
                            if cc >= 1:
                                self.w(te, "t", "v", vb_cb + cc)
                            self.inc("t", te.matmul(
                                psCb[:], P["onesrow_f"][:],
                                cbrow[:, cc * 512:(cc + 1) * 512],
                                start=True, stop=True))

                    @blk.vector
                    def _(v, tb=tb, cbsnap=cbsnap):
                        self.waitbar(v, "v", cbsnap)
                        for cc in range(4):
                            self.w(v, "v", "t", tb + cc + 1)
                            self.inc("v", v.tensor_copy(
                                cbb[:, cc * 512:(cc + 1) * 512], psCb[:]))

            for th in range(2):
                to = th * 1024
                with nc.Block("p12h_%s_%d" % (tag, th)) as blk:
                    spdb = self.cnt["spd"]
                    gb = self.cnt["g"]
                    tb = self.cnt["t"]
                    sb_ = self.cnt["s"]
                    vb = self.cnt["v"]
                    hsnap = self.bar()

                    @blk.sync
                    def _(sp, tag=tag, roff=roff, d1=d1, d3=d3, gb=gb):
                        self.waitbar(sp, "sp")
                        for ft in range(NJC):
                            for wi, dd in ((0, d1), (1, d3)):
                                self.w(sp, "sp", "g", gb + ft * 2 + wi)
                                ap = bass.AP(dd, roff * C + ft * 128,
                                             [[C, 128], [128 * C, NJC],
                                              [1, 128]])
                                self.inc("spd", sp.dma_start(
                                    out=stgt[:], in_=ap), 16)

                    @blk.gpsimd
                    def _(g, tag=tag, spdb=spdb, tb=tb):
                        for ft in range(NJC):
                            for wi in range(2):
                                self.w(g, "g", "spd",
                                       spdb + (ft * 2 + wi + 1) * 16)
                                if ft * 2 + wi >= 2:
                                    self.w(g, "g", "t", tb + ft * 2 + wi - 1)
                                dst = (w1b if wi == 0 else w3b)[ft % 2]
                                self.inc("g", g.tensor_copy(dst[:],
                                                            stgt[:]))

                    @blk.tensor
                    def _(te, tag=tag, th=th, to=to, gb=gb, sb_=sb_, vb=vb,
                          hsnap=hsnap):
                        self.waitbar(te, "t", hsnap)
                        for ft in range(NJC):
                            self.w(te, "t", "g", gb + ft * 2 + 2)
                            if ft >= 1:
                                self.w(te, "t", "s", sb_ + ft)
                                self.w(te, "t", "v", vb + ft)
                            for ci in range(NJC):
                                for cc in range(2):
                                    ins = te.matmul(
                                        psH1[:, cc * 512:(cc + 1) * 512],
                                        w1b[ft % 2][:, ci, :],
                                        nx2[:, ci, to + cc * 512:
                                            to + (cc + 1) * 512],
                                        start=(ci == 0),
                                        stop=(ci == NJC - 1))
                            self.inc("t", ins)
                            for ci in range(NJC):
                                for cc in range(2):
                                    ins = te.matmul(
                                        psH3[:, cc * 512:(cc + 1) * 512],
                                        w3b[ft % 2][:, ci, :],
                                        nx2[:, ci, to + cc * 512:
                                            to + (cc + 1) * 512],
                                        start=(ci == 0),
                                        stop=(ci == NJC - 1))
                            self.inc("t", ins)

                    @blk.scalar
                    def _(sc, tag=tag, tb=tb, vb=vb):
                        for ft in range(NJC):
                            self.w(sc, "s", "t", tb + ft * 2 + 1)
                            if ft >= 1:
                                self.w(sc, "s", "v", vb + ft)
                            self.inc("s", sc.activation(
                                self.T1[:, 0:1024], psH1[:, :], AF.Silu))

                    @blk.vector
                    def _(v, tag=tag, tb=tb, sb_=sb_):
                        for ft in range(NJC):
                            self.w(v, "v", "t", tb + ft * 2 + 2)
                            self.w(v, "v", "s", sb_ + ft + 1)
                            self.inc("v", v.tensor_mul(
                                hh[:, ft, :], psH3[:, :],
                                self.T1[:, 0:1024]))

                with nc.Block("p12o_%s_%d" % (tag, th)) as blk:
                    spdb = self.cnt["spd"]
                    gb = self.cnt["g"]
                    tb = self.cnt["t"]
                    vb = self.cnt["v"]
                    osnap = self.bar()

                    @blk.sync
                    def _(sp, tag=tag, roff=roff, d2=d2, gb=gb):
                        self.waitbar(sp, "sp")
                        for dt in range(NJC):
                            self.w(sp, "sp", "g", gb + dt)
                            ap = bass.AP(d2, roff * C + dt * 128,
                                         [[C, 128], [128 * C, NJC],
                                          [1, 128]])
                            self.inc("spd", sp.dma_start(out=stgt[:],
                                                         in_=ap), 16)

                    @blk.gpsimd
                    def _(g, tag=tag, spdb=spdb, tb=tb):
                        for dt in range(NJC):
                            self.w(g, "g", "spd", spdb + (dt + 1) * 16)
                            if dt >= 2:
                                self.w(g, "g", "t", tb + dt - 1)
                            self.inc("g", g.tensor_copy(w2b[dt % 2][:],
                                                        stgt[:]))

                    @blk.tensor
                    def _(te, tag=tag, th=th, to=to, gb=gb, vb=vb,
                          osnap=osnap):
                        self.waitbar(te, "t", osnap)
                        for dt in range(NJC):
                            self.w(te, "t", "g", gb + dt + 1)
                            if dt >= 1:
                                self.w(te, "t", "v", vb + dt)
                            for ft in range(NJC):
                                for cc in range(2):
                                    ins = te.matmul(
                                        psO[dt % 2][:, cc * 512:
                                                    (cc + 1) * 512],
                                        w2b[dt % 2][:, ft, :],
                                        hh[:, ft, cc * 512:(cc + 1) * 512],
                                        start=(ft == 0),
                                        stop=(ft == NJC - 1))
                            self.inc("t", ins)

                    @blk.vector
                    def _(v, tag=tag, th=th, to=to, tb=tb):
                        for dt in range(NJC):
                            self.w(v, "v", "t", tb + dt + 1)
                            dst = spec[:, dt, to:to + 1024]
                            if tag == "sh":
                                self.inc("v", v.tensor_copy(
                                    dst, psO[dt % 2][:, :]))
                            else:
                                v.tensor_mul(tmpo[:, :], psO[dt % 2][:, :],
                                             cbb[:, to:to + 1024])
                                self.inc("v", v.tensor_add(dst, dst,
                                                           tmpo[:, :]))
        es12.close()
        pool.free("hh", "w1b0", "w1b1", "w3b0", "w3b1", "w2b0", "w2b1",
                  "stgt", "tmpo", "nx2")
        self.tap("spec", lambda: spec[:], [128, NJC, TOK], F32)

    def p13_out(self):
        nc, D, P = self.nc, self.D, self.P
        pool = self.pool
        spec = self.spec
        xbf = pool.alloc("p13xbf", [128, NJC, TOK], BF16)
        sqbf = pool.alloc("p13sqbf", [128, NJC, TOK], BF16)
        mb, rb = self.MB, self.RB

        with nc.Block("p13f") as blk:
            guard = self.bar()
            lds = []
            vb0 = self.cnt["v"]

            @blk.sync
            def _(sp):
                self.waitbar(sp, "sp", guard)
                for jc in range(NJC):
                    if jc >= 1:
                        self.w(sp, "sp", "v", vb0 + jc)
                    lds.append(self.inc("spd", sp.dma_start(
                        out=self.T1[:, 0:TOK], in_=D["dr_xb"][:, jc, :]),
                        16))

            @blk.vector
            def _(v):
                self.waitbar(v, "v", guard)
                for jc in range(NJC):
                    self.w(v, "v", "spd", lds[jc])
                    v.tensor_add(spec[:, jc, :], spec[:, jc, :],
                                 self.T1[:, 0:TOK])
                    v.tensor_copy(xbf[:, jc, :], spec[:, jc, :])
                    self.inc("v", v.tensor_mul(sqbf[:, jc, :],
                                               xbf[:, jc, :], xbf[:, jc, :]))
        self.ln_stats("p13", TOK, xbf, sqbf, mb, rb, EPS_LN,
                      pre_snap=self.bar())
        with nc.Block("p13o") as blk:
            guard = self.bar()
            lds = []
            vb0 = self.cnt["v"]

            @blk.sync
            def _(sp):
                self.waitbar(sp, "sp", guard)
                for jc in range(NJC):
                    if jc >= 1:
                        self.w(sp, "sp", "v", vb0 + jc)
                    lds.append(self.inc("spd", sp.dma_start(
                        out=self.T1[:, 0:TOK], in_=D["dr_x1"][:, jc, :]),
                        16))

            @blk.vector
            def _(v):
                self.waitbar(v, "v", guard)
                for jc in range(NJC):
                    self.w(v, "v", "spd", lds[jc])
                    v.tensor_sub(self.T2[:, 0:TOK], spec[:, jc, :],
                                 mb[:, 0:TOK])
                    v.tensor_mul(self.T2[:, 0:TOK], self.T2[:, 0:TOK],
                                 rb[:, 0:TOK])
                    self.inc("v", v.tensor_add(spec[:, jc, :],
                                               self.T2[:, 0:TOK],
                                               self.T1[:, 0:TOK]))
            odone_v = self.cnt["v"]

            @blk.sync
            def _(sp):
                for jc in range(NJC):
                    self.w(sp, "sp", "v", vb0 + jc + 1)
                    self.inc("spd", sp.dma_start(out=D["out"][:, jc, :],
                                                 in_=spec[:, jc, :]), 16)
        pool.free("p13xbf", "p13sqbf", "spec")


_PROG_CACHE = {}


def _get_prog(taps=(), stop=None):
    key = (tuple(sorted(taps)), stop)
    if key not in _PROG_CACHE:
        kb = KB(taps=key[0], stop=stop)
        _PROG_CACHE[key] = kb.build()
    return _PROG_CACHE[key]


def _pv(vec):
    v = np.asarray(vec, np.float32).reshape(NJC, 128)
    return np.ascontiguousarray(v.T)


def _wT(w):
    return np.ascontiguousarray(np.asarray(w, np.float32).T)


def _feat_major(xt_slice):
    return np.ascontiguousarray(
        xt_slice.reshape(NJC, 128, xt_slice.shape[1]).transpose(1, 0, 2))


def make_inmaps(inputs):
    g = lambda k: np.asarray(inputs[k], np.float32)
    shared = {
        "pv_tmk": _pv(g("L0_attn_tmk")),
        "pv_tmv": _pv(g("L0_attn_tmv")),
        "pv_tmr": _pv(g("L0_attn_tmr")),
        "pv_ftmk": _pv(g("L0_attn_f_tmk")),
        "pv_ftmr": _pv(g("L0_attn_f_tmr")),
        "pv_ln1w": _pv(g("L0_attn_ln1_w")),
        "pv_ln1b": _pv(g("L0_attn_ln1_b")),
        "pv_ln2w": _pv(g("L0_attn_ln2_w")),
        "pv_ln2b": _pv(g("L0_attn_ln2_b")),
        "pv_g1": _pv(g("L0_ffn_g1")),
        "pv_g2": _pv(g("L0_ffn_g2")),
        "pv_decay": _pv(g("L0_attn_time_decay")),
        "pv_tfirst": _pv(g("L0_attn_time_first")),
        "ident": np.eye(128, dtype=np.float32),
        "ebias": np.ascontiguousarray(
            g("L0_ffn_expert_bias").reshape(1, 8)),
        "w_wk": _wT(g("L0_attn_att_kw")),
        "w_wv": _wT(g("L0_attn_att_vw")),
        "w_wr": _wT(g("L0_attn_att_rw")),
        "w_wo": _wT(g("L0_attn_att_ow")),
        "w_fkw": _wT(g("L0_attn_f_kw")),
        "w_fvw": _wT(g("L0_attn_f_vw")),
        "w_frw": _wT(g("L0_attn_f_rw")),
        "w_wq": _wT(g("L0_ffn_Wq")),
        "w_wkc": _wT(g("L0_ffn_Wkc")),
        "w_wvc": _wT(g("L0_ffn_Wvc")),
        "w_wwo": _wT(g("L0_ffn_Wo")),
        "w_wwr": _wT(g("L0_ffn_Wr")),
        "w_s1": _wT(g("L0_ffn_sW1")),
        "w_s2": _wT(g("L0_ffn_sW2")),
        "w_s3": _wT(g("L0_ffn_sW3")),
        "w_e1": np.ascontiguousarray(g("L0_ffn_eW1").transpose(0, 2, 1)),
        "w_e2": np.ascontiguousarray(g("L0_ffn_eW2").transpose(0, 2, 1)),
        "w_e3": np.ascontiguousarray(g("L0_ffn_eW3").transpose(0, 2, 1)),
    }
    x = g("L0_attn_x")
    in_maps = []
    for core in range(8):
        b, r = divmod(core, 4)
        s0 = r * TOK
        xt = x[b].T
        bm = np.zeros((128, 2), np.float32)
        if r == 0:
            sl = np.concatenate([xt[:, 0:1], xt[:, 0:1], xt[:, 0:TOK]],
                                axis=1)
            bm[:, 1] = -100.0
        else:
            sl = xt[:, s0 - 2:s0 + TOK]
            bm[:, 0] = 1.0
        sel = np.zeros((128, 4), np.float32)
        sel[:, r] = 1.0
        m = dict(shared)
        m["x"] = _feat_major(np.ascontiguousarray(sl))
        m["sel"] = sel
        m["bm"] = bm
        in_maps.append(m)
    return in_maps


def kernel(**inputs):
    prog = _get_prog()
    in_maps = make_inmaps(inputs)
    res = run_bass_kernel_spmd(prog, in_maps, core_ids=list(range(8)))
    out = np.empty((2, 8192, 1024), np.float32)
    for core in range(8):
        b, r = divmod(core, 4)
        o = np.asarray(res.results[core]["out"])
        out[b, r * TOK:(r + 1) * TOK, :] = \
            o.transpose(1, 0, 2).reshape(1024, TOK).T
    return out
